# version 11 (final)
# baseline (speedup 1.0000x reference)
import numpy as np
import ml_dtypes
from contextlib import ExitStack

import concourse.bass as bass
import concourse.mybir as mybir
from concourse.bass_utils import run_bass_kernel_spmd

F32 = mybir.dt.float32
BF16 = mybir.dt.bfloat16
AX = mybir.AxisListType.X
ALU = mybir.AluOpType
ACT = mybir.ActivationFunctionType

T = 2048
D = 1024
NQT = T // 128
NDC = D // 128
LAT = 32
E = 8
HE = 512
SCALE = 128.0 ** -0.5
EPS = 1e-5

_CACHE = {}


class Buf:
    __slots__ = ("w", "r")

    def __init__(self):
        self.w = None
        self.r = []


class Prog:
    ENGS = ("t", "v", "s", "g", "q")

    DSEMS = ("Dx", "D0", "D1", "D2", "D3", "D4", "D5", "D6", "D7", "E0", "E1", "Ds", "Do", "Dc")

    def __init__(self, nc, ctx):
        self.nc = nc
        self.ops = {k: [] for k in self.ENGS}
        self.sem = {k: ctx.enter_context(nc.semaphore("sem_" + k)) for k in self.ENGS}
        self.dsem = {k: ctx.enter_context(nc.semaphore("dma_" + k)) for k in self.DSEMS}
        self.cnt = {k: 0 for k in self.ENGS}
        self.dcnt = {k: 0 for k in self.DSEMS}
        self.waited = {k: {} for k in self.ENGS}

    def _sem(self, sk):
        return self.dsem[sk] if sk in self.dsem else self.sem[sk]

    def _collect(self, k, rd, wr, extra):
        toks = []
        for b in rd:
            if b.w:
                toks.append(b.w)
        for b in wr:
            toks.extend(b.r)
            if b.w:
                toks.append(b.w)
        toks.extend(t for t in extra if t is not None)
        out = []
        seen = self.waited[k]
        for (sk, v) in toks:
            if seen.get(sk, 0) >= v:
                continue
            seen[sk] = v
            out.append((sk, v))
        return out

    def op(self, k, build, rd=(), wr=(), inc=True, extra=(), reg=True):
        wl = self._collect(k, rd, wr, extra)
        tok = None
        if inc:
            self.cnt[k] += 1
            tok = (k, self.cnt[k])

        def thunk(eng, build=build, wl=wl, inc=inc, k=k, self=self):
            for (sk, v) in wl:
                eng.wait_ge(self._sem(sk), v)
            inst = build(eng)
            if inc:
                inst.then_inc(self.sem[k], 1)

        self.ops[k].append(thunk)
        if tok and reg:
            for b in rd:
                b.r.append(tok)
            for b in wr:
                b.w = tok
                b.r = []
        return tok

    def dma(self, k, build, rd=(), wr=(), extra=(), sem="Ds", serial=None,
            token_val=None):
        if serial is None:
            serial = (sem == "Ds")
        wl = self._collect(k, rd, wr, extra)
        if serial and self.dcnt[sem] > 0:
            pre = (sem, self.dcnt[sem])
            seen = self.waited[k]
            if seen.get(sem, 0) < pre[1]:
                seen[sem] = pre[1]
                wl = wl + [pre]
        self.dcnt[sem] += 16
        tok = (sem, self.dcnt[sem] if token_val is None else token_val)

        def thunk(eng, build=build, wl=wl, sem=sem, self=self):
            for (sk, v) in wl:
                eng.wait_ge(self._sem(sk), v)
            build(eng).then_inc(self.dsem[sem], 16)

        self.ops[k].append(thunk)
        for b in rd:
            b.r.append(tok)
        for b in wr:
            b.w = tok
            b.r = []
        return tok

    def reg_group(self, tok, rd=(), wr=()):
        for b in rd:
            b.r.append(tok)
        for b in wr:
            b.w = tok
            b.r = []

    def finalize(self):
        nc = self.nc
        final = [(k, self.cnt[k]) for k in self.ENGS if self.cnt[k]]
        final += [(s, self.dcnt[s]) for s in self.DSEMS if self.dcnt[s]]
        with nc.Block() as block:

            @block.tensor
            def _(tensor):
                for f in self.ops["t"]:
                    f(tensor)

            @block.vector
            def _(vector):
                for f in self.ops["v"]:
                    f(vector)

            @block.scalar
            def _(scalar):
                for f in self.ops["s"]:
                    f(scalar)

            @block.gpsimd
            def _(gpsimd):
                for f in self.ops["g"]:
                    f(gpsimd)

            @block.sync
            def _(sync):
                for f in self.ops["q"]:
                    f(sync)
                for (sk, v) in final:
                    sync.wait_ge(self._sem(sk), v)


def kchunks(j):
    out = []
    for c in range(j // 4, 4):
        lo = max(512 * c, 128 * j)
        out.append((lo, 512 * (c + 1) - lo, c))
    return out


def build_program():
    nc = bass.Bass("TRN2", target_bir_lowering=False)

    dx = nc.dram_tensor("x", [T, D], F32, kind="ExternalInput")
    dwq = nc.dram_tensor("wq", [D, D], F32, kind="ExternalInput")
    dwkd = nc.dram_tensor("wk_down", [D, 64], F32, kind="ExternalInput")
    dwvd = nc.dram_tensor("wv_down", [D, 64], F32, kind="ExternalInput")
    dwku = nc.dram_tensor("wk_up", [LAT, 128], F32, kind="ExternalInput")
    dwvu = nc.dram_tensor("wv_up", [LAT, 128], F32, kind="ExternalInput")
    dwo = nc.dram_tensor("wo", [D, D], F32, kind="ExternalInput")
    dwg = nc.dram_tensor("w_g", [D, E], F32, kind="ExternalInput")
    dwgate = nc.dram_tensor("w_gate", [E, D, HE], F32, kind="ExternalInput")
    dwup = nc.dram_tensor("w_up", [E, D, HE], F32, kind="ExternalInput")
    dwdown = nc.dram_tensor("w_down", [E, HE, D], F32, kind="ExternalInput")
    dwsg = nc.dram_tensor("w_sg", [D, HE], F32, kind="ExternalInput")
    dwsu = nc.dram_tensor("w_su", [D, HE], F32, kind="ExternalInput")
    dwsd = nc.dram_tensor("w_sd", [HE, D], F32, kind="ExternalInput")
    dc_in = nc.dram_tensor("c_row", [D], F32, kind="ExternalInput")
    dwmod = nc.dram_tensor("wmod", [D, 6 * D], F32, kind="ExternalInput")
    dy = nc.dram_tensor("y", [T, D], F32, kind="ExternalOutput")

    ident_np = np.eye(128, dtype=np.float32).astype(ml_dtypes.bfloat16)
    ii, jj = np.indices((128, 128))
    mask01_np = (jj >= ii).astype(np.float32).astype(ml_dtypes.bfloat16)
    ob2_np = np.zeros((128, 2), np.float32)
    ob2_np[0:32, 0] = 1.0
    ob2_np[64:96, 1] = 1.0
    ones1_np = np.ones((1, 128), np.float32).astype(ml_dtypes.bfloat16)
    ident32_np = np.eye(128, dtype=np.float32)
    d_ident = nc.inline_tensor(ident_np, name="c_ident")
    d_ident32 = nc.inline_tensor(ident32_np, name="c_ident32")
    d_ones1 = nc.inline_tensor(ones1_np, name="c_ones1")
    d_mask = nc.inline_tensor(mask01_np, name="c_mask01")
    d_ob2 = nc.inline_tensor(ob2_np.astype(ml_dtypes.bfloat16), name="c_ob2")

    ctx = ExitStack()
    with ctx:
        def sb(name, shape, dt):
            return nc.alloc_sbuf_tensor(name, shape, dt)

        x_res = sb("x_res", [128, NQT, D], F32)
        hT = sb("hT", [128, NDC, T], BF16)
        ca1 = sb("ca1", [128, D], F32)
        sab = sb("sab", [128, D], BF16)
        gbv = sb("gbv", [128, D], BF16)
        stage = sb("stage", [128, 2, 512], F32)
        comb = sb("comb", [128, NQT, E], F32)
        st_sm = sb("st_sm", [128, NQT], F32)
        st_ss = sb("st_ss", [128, NQT], F32)
        st_nm = sb("st_nm", [128, NQT], F32)
        st_var = sb("st_var", [128, NQT], F32)
        st_rs = sb("st_rs", [128, NQT], F32)
        st_nmr = sb("st_nmr", [128, NQT], F32)
        c_sb = sb("c_sb", [128, 8], F32)
        c_si = sb("c_si", [128, 8], BF16)
        c_si32 = sb("c_si32", [128, 8], F32)
        ident = sb("ident", [128, 128], BF16)
        ident32 = sb("ident32", [128, 128], F32)
        mask01 = sb("mask01", [128, 128], BF16)
        ob2 = sb("ob2", [128, 2], BF16)
        ones1 = sb("ones1", [1, 128], BF16)
        rd_bf = sb("rd_bf", [1, 512], BF16)
        mod_bf1 = sb("mod_bf1", [1, 512], BF16)
        wg_bf = sb("wg_bf", [128, NDC, 8], BF16)
        h1b = sb("h1b", [128, D], F32)
        htok = sb("htok", [128, D], BF16)
        r_t1 = sb("r_t1", [128, 8], F32)
        r_t2 = sb("r_t2", [128, 8], F32)
        r_t3 = sb("r_t3", [128, 8], F32)
        r_sc = sb("r_sc", [128, 1], F32)
        knm_p = sb("knm_p", [1, 4], F32)
        knm1 = sb("knm1", [1, 1], F32)
        knm_gg = sb("knm_gg", [2, 2], F32)

        AR = 90368
        base, _top = nc.bump_sbuf(AR)

        def at(name, shape, dt, off):
            return nc.alloc_sbuf_tensor_at(name, shape, dt, offset=base + off)

        q_aug = at("q_aug", [128, 4, T], BF16, 0)
        act_e = at("act_e", [128, 4, T], BF16, 0)
        wq_aug = at("wq_aug", [128, NDC, 4, 128], BF16, 16384)
        q2 = at("q2", [128, T], BF16, 16384)
        k2 = at("k2", [128, T], BF16, 20480)
        pT = at("pT", [128, 2, T], BF16, 16384)
        z_sb = at("z_sb", [128, 2, T], BF16, 24576)
        bbf = at("bbf", [2, T], BF16, 24576)
        nrm2 = at("nrm2", [2, 512], F32, 28672)
        wbf = at("wbf", [128, 2, D], BF16, 32768)
        k_aug = at("k_aug", [128, T], BF16, 36864)
        va = at("va", [128, 16, 66], BF16, 40960)
        wo_pk = at("wo_pk", [128, 2, D], BF16, 43136)
        wkv_bf = at("wkv_bf", [128, NDC, 128], BF16, 47232)
        wv_bf = at("wv_bf", [128, NDC, 64], BF16, 49280)
        wqT = at("wqT", [128, 2, 128], BF16, 50304)
        wku_s = at("wku_s", [32, 2, 128], F32, 51456)
        wupT = at("wupT", [128, 2, 32], BF16, 51328)
        rd_s = at("rd_s", [128, 512], F32, 53504)
        rfac = at("rfac", [128, 512], F32, 55552)
        silu_s = at("silu_s", [128, 2, 512], BF16, 53504)
        gate_bf = at("gate_bf", [128, 2, NDC, 512], BF16, 57600)
        stage2 = at("stage2", [128, 8, 512], F32, 57600)
        up_bf = at("up_bf", [128, NDC, 512], BF16, 73984)
        down_bf = at("down_bf", [128, 4, D], BF16, 82176)

        psf = [ctx.enter_context(nc.psum_tensor("ps%d" % i, [128, 512], F32))
               for i in range(8)]
        psT = [psf[6], psf[7]]

        P = Prog(nc, ctx)
        NB = lambda: Buf()

        b_x = [NB() for _ in range(NQT)]
        b_hT = [NB() for _ in range(NQT)]
        b_ca1, b_sab, b_gbv = NB(), NB(), NB()
        b_stage = [NB(), NB()]
        b_st2 = [NB() for _ in range(8)]
        b_gate = [NB(), NB()]
        b_up = NB()
        b_down = NB()
        b_comb, b_stats = NB(), NB()
        b_c, b_csi, b_csi32 = NB(), NB(), NB()
        b_mod = [NB(), NB()]
        b_cst = NB()
        b_cst32 = NB()
        b_msk = NB()
        b_ob2c = NB()
        b_ones = NB()
        b_rdbf = NB()
        b_modbf1 = NB()
        b_wg = NB()
        b_h1, b_htk = NB(), NB()
        b_h1h = [NB(), NB()]
        b_wbf = [NB(), NB()]
        b_qaug = [NB() for _ in range(4)]
        b_kaug, b_va = NB(), NB()
        b_pT = [NB(), NB()]
        b_z = [NB(), NB()]
        b_wqaug, b_wopk = NB(), NB()
        b_wkv, b_wv = NB(), NB()
        b_wqT = [NB(), NB()]
        b_wupT, b_wkus = NB(), NB()
        b_q2, b_nrm2, b_k2 = NB(), NB(), NB()
        b_knmp, b_knm1, b_knmgg = NB(), NB(), NB()
        b_bbf = NB()
        b_rd, b_rfac = NB(), NB()
        b_act = NB()
        b_rt = NB()
        b_silu = [NB(), NB()]
        b_ps = [NB() for _ in range(8)]
        b_pt = [b_ps[6], b_ps[7]]

        _psi = [0]

        def next_ps():
            i = _psi[0] % 6
            _psi[0] += 1
            return psf[i], b_ps[i]

        P.dma("g", lambda e: e.dma_start(out=ident[:], in_=d_ident[:, :]), wr=[b_cst], sem="Dc", serial=False, token_val=96)
        P.dma("g", lambda e: e.dma_start(out=ident32[:], in_=d_ident32[:, :]), wr=[b_cst32], sem="Dc", serial=False, token_val=96)
        P.dma("g", lambda e: e.dma_start(out=mask01[:], in_=d_mask[:, :]), wr=[b_msk], sem="Dc", serial=False, token_val=96)
        P.dma("g", lambda e: e.dma_start(out=ob2[:], in_=d_ob2[:, :]), wr=[b_ob2c], sem="Dc", serial=False, token_val=96)
        P.dma("g", lambda e: e.dma_start(out=ones1[:], in_=d_ones1[:, :]), wr=[b_ones], sem="Dc", serial=False, token_val=96)
        P.dma("g", lambda e: e.dma_start(
            out=c_sb[:], in_=dc_in.ap().rearrange("(k p) -> p k", p=128),
            allow_slow_non_contiguous=True), wr=[b_c], sem="Dc", serial=False,
            token_val=96)
        for i in range(NQT):
            P.dma("q", lambda e, i=i: e.dma_start(
                out=x_res[:, i, :], in_=dx[128 * i:128 * (i + 1), :]),
                wr=[b_x[i]], sem="Dx", serial=False, token_val=16 * NQT)

        P.op("s", lambda e: e.activation(c_si[:], c_sb[:], ACT.Silu),
             rd=[b_c], wr=[b_csi])
        P.op("s", lambda e: e.activation(c_si32[:], c_sb[:], ACT.Silu),
             rd=[b_c], wr=[b_csi32])

        def mod_chunk(n):
            vsel, half = n // 2, n % 2
            cols = slice(512 * half, 512 * (half + 1))
            ps, b_p = next_ps()
            tok = None
            for k in range(NDC):
                stg = (n * NDC + k) % 8
                P.dma("s" if stg % 2 else "q", lambda e, n=n, k=k, stg=stg: e.dma_start(
                    out=stage2[:, stg, :],
                    in_=dwmod[128 * k:128 * (k + 1), 512 * n:512 * (n + 1)]),
                    wr=[b_st2[stg]], sem="D%d" % stg, serial=False)
                tok = P.op("t", lambda e, k=k, stg=stg, ps=ps: e.matmul(
                    ps[0:1, :], c_si32[:, k:k + 1], stage2[:, stg, :],
                    start=(k == 0), stop=(k == NDC - 1)),
                    rd=[b_csi32, b_st2[stg]],
                    wr=([b_p] if k == 0 else ()),
                    inc=True, reg=False)
                b_st2[stg].r.append(tok)
            P.reg_group(tok, rd=[b_csi32], wr=[b_p])
            P.op("v", lambda e, ps=ps: e.tensor_copy(
                mod_bf1[0:1, :], ps[0:1, :]), rd=[b_p], wr=[b_modbf1])
            bps, b_bp = next_ps()
            P.op("t", lambda e, bps=bps: e.matmul(
                bps[:, :], ones1[0:1, :], mod_bf1[0:1, :], start=True, stop=True),
                rd=[b_modbf1, b_ones], wr=[b_bp])
            if vsel in (1, 4):
                P.op("v", lambda e, cols=cols, bps=bps: e.tensor_scalar_add(
                    ca1[:, cols], bps[:, :], 1.0), rd=[b_bp], wr=[b_ca1])
            elif vsel in (0, 3):
                P.op("v", lambda e, cols=cols, bps=bps: e.tensor_copy(
                    sab[:, cols], bps[:, :]), rd=[b_bp], wr=[b_sab])
            else:
                P.op("v", lambda e, bps=bps: e.tensor_scalar(
                    rd_s[:, :], bps[:, :], 0.0, None, ALU.is_ge),
                    rd=[b_bp], wr=[b_rd])
                P.op("v", lambda e: e.tensor_scalar(
                    rd_s[:, :], rd_s[:, :], 2.0, -1.0, ALU.mult, ALU.add),
                    rd=[b_rd], wr=[b_rd])
                P.op("v", lambda e, bps=bps: e.tensor_tensor(
                    rfac[:, :], bps[:, :], rd_s[:, :], ALU.mult),
                    rd=[b_bp, b_rd], wr=[b_rfac])
                P.op("v", lambda e: e.tensor_scalar_max(
                    rfac[:, :], rfac[:, :], 0.25),
                    rd=[b_rfac], wr=[b_rfac])
                P.op("v", lambda e, cols=cols: e.tensor_tensor(
                    gbv[:, cols], rd_s[:, :], rfac[:, :], ALU.mult),
                    rd=[b_rd, b_rfac], wr=[b_gbv])

        for n in range(6):
            mod_chunk(n)

        def layernorm_phase():
            for grp in range(4):
                gs = slice(4 * grp, 4 * grp + 4)
                for i in range(4 * grp, 4 * grp + 4):
                    P.op("s", lambda e, i=i: e.activation(
                        h1b[:], x_res[:, i, :], ACT.Square,
                        accum_out=st_ss[:, i:i + 1]),
                        rd=[b_x[i]], wr=[b_h1, b_h1h[0], b_h1h[1], b_stats])
                    P.op("v", lambda e, i=i: e.reduce_sum(
                        st_sm[:, i:i + 1], x_res[:, i, :], axis=AX),
                        rd=[b_x[i]], wr=[b_stats])
                P.op("v", lambda e, gs=gs: e.tensor_scalar_mul(
                    st_nm[:, gs], st_sm[:, gs], -1.0 / D),
                    rd=[b_stats], wr=[b_stats])
                P.op("v", lambda e, gs=gs: e.tensor_scalar(
                    st_var[:, gs], st_ss[:, gs], 1.0 / D, EPS, ALU.mult, ALU.add),
                    rd=[b_stats], wr=[b_stats])
                P.op("v", lambda e, gs=gs: e.tensor_tensor(
                    st_nmr[:, gs], st_nm[:, gs], st_nm[:, gs], ALU.mult),
                    rd=[b_stats], wr=[b_stats])
                P.op("v", lambda e, gs=gs: e.tensor_tensor(
                    st_var[:, gs], st_var[:, gs], st_nmr[:, gs], ALU.subtract),
                    rd=[b_stats], wr=[b_stats])
                P.op("s", lambda e, gs=gs: e.activation(
                    st_rs[:, gs], st_var[:, gs], ACT.Sqrt),
                    rd=[b_stats], wr=[b_stats])
                P.op("v", lambda e, gs=gs: e.reciprocal(st_rs[:, gs], st_rs[:, gs]),
                     rd=[b_stats], wr=[b_stats])
                P.op("v", lambda e, gs=gs: e.tensor_tensor(
                    st_nmr[:, gs], st_nm[:, gs], st_rs[:, gs], ALU.mult),
                    rd=[b_stats], wr=[b_stats])
                for i in range(4 * grp, 4 * grp + 4):
                    for half in range(2):
                        hs = slice(512 * half, 512 * (half + 1))
                        b_hh = b_h1h[half]
                        P.op("s", lambda e, i=i, hs=hs: e.activation(
                            h1b[:, hs], x_res[:, i, hs], ACT.Identity,
                            bias=st_nmr[:, i:i + 1], scale=st_rs[:, i:i + 1]),
                            rd=[b_x[i], b_stats], wr=[b_hh])
                        P.op("v", lambda e, hs=hs: e.tensor_tensor(
                            h1b[:, hs], h1b[:, hs], ca1[:, hs], ALU.mult),
                            rd=[b_hh, b_ca1], wr=[b_hh])
                        P.op("v", lambda e, hs=hs: e.tensor_tensor(
                            h1b[:, hs], h1b[:, hs], sab[:, hs], ALU.add),
                            rd=[b_hh, b_sab], wr=[b_hh])
                        pt, b_p = psT[half], b_pt[half]
                        tok = None
                        for u in range(4):
                            dc = half * 4 + u
                            tok = P.op("t", lambda e, dc=dc, u=u, pt=pt: e.transpose(
                                pt[:, 128 * u:128 * (u + 1)],
                                h1b[:, 128 * dc:128 * (dc + 1)], ident32[:]),
                                rd=[b_hh, b_cst32], wr=([b_p] if u == 0 else ()),
                                inc=(u == 3), reg=False)
                        P.reg_group(tok, rd=[b_hh, b_cst32], wr=[b_p])
                        P.op("v", lambda e, i=i, half=half, pt=pt: e.tensor_copy(
                            hT[:, half * 4:half * 4 + 4, 128 * i:128 * (i + 1)],
                            pt[:, 0:512].rearrange("p (a b) -> p a b", a=4)),
                            rd=[b_p], wr=[b_hT[i]])

        layernorm_phase()

        P.dma("s", lambda e: e.dma_start(out=stage[0:32, 0, 0:128], in_=dwku[:, :]),
              wr=[b_stage[0]], sem="E0", serial=False)
        P.dma("s", lambda e: e.dma_start(out=stage[0:32, 1, 0:128], in_=dwvu[:, :]),
              wr=[b_stage[1]], sem="E1", serial=False)
        P.op("v", lambda e: e.tensor_scalar_mul(wku_s[:, 0, :], stage[0:32, 0, 0:128], SCALE),
             rd=[b_stage[0]], wr=[b_wkus])
        P.op("v", lambda e: e.tensor_copy(wku_s[:, 1, :], stage[0:32, 1, 0:128]),
             rd=[b_stage[1]], wr=[b_wkus])
        tok = None
        for s_ in range(2):
            tok = P.op("t", lambda e, s_=s_: e.transpose(
                psT[0][:, 32 * s_:32 * (s_ + 1)], wku_s[:, s_, :], ident32[0:32, 0:32]),
                rd=[b_wkus, b_cst32], wr=([b_pt[0]] if s_ == 0 else ()),
                inc=(s_ == 1), reg=False)
        P.reg_group(tok, rd=[b_wkus, b_cst], wr=[b_pt[0]])
        P.op("v", lambda e: e.tensor_copy(
            wupT[:, :, :].rearrange("p a b -> p (a b)"), psT[0][:, 0:64]),
            rd=[b_pt[0]], wr=[b_wupT])

        P.op("v", lambda e: e.memset(wq_aug[:], 0.0), wr=[b_wqaug])
        for dc in range(NDC):
            pr2 = 2 * (dc % 4)
            P.dma("s", lambda e, dc=dc, pr2=pr2: e.dma_start(
                out=stage2[:, pr2:pr2 + 2, :].rearrange("p a b -> p (a b)"),
                in_=dwq[128 * dc:128 * (dc + 1), :]),
                wr=[b_st2[pr2], b_st2[pr2 + 1]], sem="D%d" % pr2, serial=False)
            for h in range(8):
                tsl = h % 2
                P.op("t", lambda e, h=h, tsl=tsl, pr2=pr2: e.transpose(
                    psT[tsl][:, 0:128],
                    stage2[:, pr2:pr2 + 2, :].rearrange("p a b -> p (a b)")[:, 128 * h:128 * (h + 1)],
                    ident32[:]),
                    rd=[b_st2[pr2], b_st2[pr2 + 1], b_cst32], wr=[b_pt[tsl]])
                P.op("v", lambda e, tsl=tsl: e.tensor_copy(
                    wqT[:, tsl, :], psT[tsl][:, 0:128]),
                    rd=[b_pt[tsl]], wr=[b_wqT[tsl]])
                ps, b_p = next_ps()
                P.op("t", lambda e, tsl=tsl, ps=ps: e.matmul(
                    ps[:, 0:32], wqT[:, tsl, :], wupT[:, 0, :],
                    start=True, stop=True),
                    rd=[b_wqT[tsl], b_wupT], wr=[b_p])
                P.op("v", lambda e, dc=dc, h=h, ps=ps: e.tensor_copy(
                    wq_aug[:, dc, h % 4, 64 * (h // 4):64 * (h // 4) + 32],
                    ps[:, 0:32]),
                    rd=[b_p], wr=[b_wqaug])

        for h in range(8):
            pr2 = 4 + 2 * (h % 2)
            P.dma("s", lambda e, h=h, pr2=pr2: e.dma_start(
                out=stage2[:, pr2:pr2 + 2, :].rearrange("p a b -> p (a b)"),
                in_=dwo[128 * h:128 * (h + 1), :]),
                wr=[b_st2[pr2], b_st2[pr2 + 1]], sem="D%d" % pr2, serial=False)
            wsl = h % 2
            P.op("v", lambda e, wsl=wsl, pr2=pr2: e.tensor_copy(
                wbf[:, wsl, :], stage2[:, pr2:pr2 + 2, :].rearrange("p a b -> p (a b)")),
                rd=[b_st2[pr2], b_st2[pr2 + 1]], wr=[b_wbf[wsl]])
            hp = h % 4
            obase = 64 if hp >= 2 else 0
            for dch in range(2):
                ps, b_p = next_ps()
                P.op("t", lambda e, wsl=wsl, dch=dch, ps=ps, obase=obase: e.matmul(
                    ps[obase:obase + 32, :], wupT[:, 1, :],
                    wbf[:, wsl, 512 * dch:512 * (dch + 1)], start=True, stop=True),
                    rd=[b_wbf[wsl], b_wupT], wr=[b_p])
                P.op("v", lambda e, h=h, hp=hp, dch=dch, ps=ps, obase=obase: e.tensor_tensor(
                    wo_pk[32 * hp:32 * hp + 32, h // 4, 512 * dch:512 * (dch + 1)],
                    ps[obase:obase + 32, :],
                    gbv[obase:obase + 32, 512 * dch:512 * (dch + 1)], ALU.mult),
                    rd=[b_p, b_gbv], wr=[b_wopk])

        P.dma("s", lambda e: e.dma_start(
            out=stage[:, 0, :].rearrange("p (k n) -> p k n", k=8),
            in_=dwkd.ap().rearrange("(k p) n -> p k n", p=128)),
            wr=[b_stage[0]], sem="E0", serial=False)
        P.dma("s", lambda e: e.dma_start(
            out=stage[:, 1, :].rearrange("p (k n) -> p k n", k=8),
            in_=dwvd.ap().rearrange("(k p) n -> p k n", p=128)),
            wr=[b_stage[1]], sem="E1", serial=False)
        P.op("v", lambda e: e.memset(wkv_bf[:], 0.0), wr=[b_wkv])
        P.op("v", lambda e: e.memset(k_aug[:], 0.0), wr=[b_kaug])
        for dc in range(NDC):
            P.op("v", lambda e, dc=dc: e.tensor_copy(
                wkv_bf[:, dc, 0:32],
                stage[:, 0, 64 * dc:64 * dc + 32]),
                rd=[b_stage[0]], wr=[b_wkv])
            P.op("v", lambda e, dc=dc: e.tensor_copy(
                wkv_bf[:, dc, 64:96],
                stage[:, 0, 64 * dc + 32:64 * dc + 64]),
                rd=[b_stage[0]], wr=[b_wkv])
            P.op("v", lambda e, dc=dc: e.tensor_copy(
                wv_bf[:, dc, :], stage[:, 1, 64 * dc:64 * dc + 64]),
                rd=[b_stage[1]], wr=[b_wv])
        P.dma("s", lambda e: e.dma_start(
            out=stage[:, 0, 0:64].rearrange("p (k n) -> p k n", k=8),
            in_=dwg.ap().rearrange("(k p) n -> p k n", p=128)),
            wr=[b_stage[0]], sem="E0", serial=False)
        P.op("v", lambda e: e.tensor_copy(
            wg_bf[:].rearrange("p a b -> p (a b)"), stage[:, 0, 0:64]),
            rd=[b_stage[0]], wr=[b_wg])

        for p in range(4):
            for qc in range(4):
                ps, b_p = next_ps()
                rdl = [b_wqaug] + [b_hT[4 * qc + u] for u in range(4)]
                tok = None
                for dc in range(NDC):
                    tok = P.op("t", lambda e, p=p, qc=qc, dc=dc, ps=ps: e.matmul(
                        ps[:, :], wq_aug[:, dc, p, :],
                        hT[:, dc, 512 * qc:512 * (qc + 1)],
                        start=(dc == 0), stop=(dc == NDC - 1)),
                        rd=rdl, wr=([b_p] if dc == 0 else ()),
                        inc=(dc == NDC - 1), reg=False)
                P.reg_group(tok, rd=rdl, wr=[b_p])
                P.op("v", lambda e, p=p, qc=qc, ps=ps: e.tensor_copy(
                    q_aug[:, p, 512 * qc:512 * (qc + 1)], ps[:, :]),
                    rd=[b_p], wr=[b_qaug[p]])
        for qc in range(4):
            ps, b_p = next_ps()
            rdl = [b_wkv] + [b_hT[4 * qc + u] for u in range(4)]
            tok = None
            for dc in range(NDC):
                tok = P.op("t", lambda e, qc=qc, dc=dc, ps=ps: e.matmul(
                    ps[:, :], wkv_bf[:, dc, :], hT[:, dc, 512 * qc:512 * (qc + 1)],
                    start=(dc == 0), stop=(dc == NDC - 1)),
                    rd=rdl, wr=([b_p] if dc == 0 else ()),
                    inc=(dc == NDC - 1), reg=False)
            P.reg_group(tok, rd=rdl, wr=[b_p])
            P.op("v", lambda e, qc=qc, ps=ps: e.tensor_copy(
                k_aug[0:96, 512 * qc:512 * (qc + 1)], ps[0:96, :]),
                rd=[b_p], wr=[b_kaug])
            for iq in range(4):
                i = 4 * qc + iq
                ps2, b_p2 = next_ps()
                rdl = [b_wv, b_hT[i]]
                tok = None
                for dc in range(NDC):
                    tok = P.op("t", lambda e, i=i, dc=dc, ps2=ps2: e.matmul(
                        ps2[:, 0:64], hT[:, dc, 128 * i:128 * (i + 1)],
                        wv_bf[:, dc, :],
                        start=(dc == 0), stop=(dc == NDC - 1)),
                        rd=rdl, wr=([b_p2] if dc == 0 else ()),
                        inc=(dc == NDC - 1), reg=False)
                P.reg_group(tok, rd=rdl, wr=[b_p2])
                P.op("v", lambda e, i=i, ps2=ps2: e.tensor_copy(
                    va[:, i, 0:32], ps2[:, 0:32]), rd=[b_p2], wr=[b_va])
                P.op("v", lambda e, i=i, ps2=ps2: e.tensor_copy(
                    va[:, i, 33:65], ps2[:, 32:64]), rd=[b_p2], wr=[b_va])
        P.op("v", lambda e: e.memset(k_aug[32:33, :], -1.0), wr=[b_kaug])
        P.op("v", lambda e: e.memset(k_aug[96:97, :], -1.0), wr=[b_kaug])

        P.op("v", lambda e: e.memset(va[:, :, 32:33], 1.0), wr=[b_va])
        P.op("v", lambda e: e.memset(va[:, :, 65:66], 1.0), wr=[b_va])

        P.op("v", lambda e: e.tensor_tensor(k2[:], k_aug[:], k_aug[:], ALU.mult),
             rd=[b_kaug], wr=[b_k2])
        for g in range(2):
            for qc in range(4):
                ps, b_p = next_ps()
                P.op("t", lambda e, g=g, qc=qc, ps=ps: e.matmul(
                    ps[0:1, :], ob2[:, g:g + 1], k2[:, 512 * qc:512 * (qc + 1)],
                    start=True, stop=True),
                    rd=[b_k2, b_ob2c], wr=[b_p])
                P.op("v", lambda e, qc=qc, ps=ps: e.reduce_max(
                    knm_p[0:1, qc:qc + 1], ps[0:1, :], axis=AX),
                    rd=[b_p], wr=[b_knmp])
            P.op("v", lambda e: e.reduce_max(knm1[0:1, 0:1], knm_p[0:1, :], axis=AX),
                 rd=[b_knmp], wr=[b_knm1])
            P.dma("g", lambda e, g=g: e.dma_start(
                out=knm_gg[g:g + 1, 0:1], in_=knm1[0:1, 0:1]),
                rd=[b_knm1], wr=[b_knmgg])

        for p in range(4):
            P.op("v", lambda e, p=p: e.tensor_tensor(
                q2[:], q_aug[:, p, :], q_aug[:, p, :], ALU.mult),
                rd=[b_qaug[p]], wr=[b_q2])
            for qc in range(4):
                ps, b_p = next_ps()
                P.op("t", lambda e, qc=qc, ps=ps: e.matmul(
                    ps[0:2, :], ob2[:, :], q2[:, 512 * qc:512 * (qc + 1)],
                    start=True, stop=True),
                    rd=[b_q2, b_ob2c], wr=[b_p])
                P.op("v", lambda e, qc=qc, ps=ps: e.tensor_scalar_mul(
                    nrm2[0:2, :], ps[0:2, :], knm_gg[0:2, 0:1]),
                    rd=[b_p, b_knmgg], wr=[b_nrm2])
                P.op("s", lambda e, qc=qc: e.activation(
                    bbf[0:2, 512 * qc:512 * (qc + 1)], nrm2[0:2, :], ACT.Sqrt),
                    rd=[b_nrm2], wr=[b_bbf])
            P.dma("g", lambda e, p=p: e.dma_start(
                out=q_aug[32:33, p, :], in_=bbf[0:1, :]),
                rd=[b_bbf], wr=[b_qaug[p]])
            P.dma("g", lambda e, p=p: e.dma_start(
                out=q_aug[96:97, p, :], in_=bbf[1:2, :]),
                rd=[b_bbf], wr=[b_qaug[p]])

        z_guard = list(b_bbf.r) + list(b_nrm2.r)
        if b_bbf.w:
            z_guard.append(b_bbf.w)
        if b_nrm2.w:
            z_guard.append(b_nrm2.w)

        for n in range(6, 12):
            mod_chunk(n)

        _sti = [0]
        for pr in range(4):
            g = pr // 2
            kb = 64 * g
            for j in range(16):
                expt = {}
                for hh in range(2):
                    h = 2 * pr + hh
                    tile = h % 4
                    for (lo, w, c) in kchunks(j):
                        sbk = (0, 1, 6, 7)[_sti[0] % 4]
                        ps, b_p = psf[sbk], b_ps[sbk]
                        _sti[0] += 1
                        P.op("t", lambda e, j=j, kb=kb, tile=tile, lo=lo, w=w, ps=ps: e.matmul(
                            ps[:, 0:w], k_aug[kb:kb + 33, 128 * j:128 * (j + 1)],
                            q_aug[kb:kb + 33, tile, lo:lo + w],
                            start=True, stop=True),
                            rd=[b_kaug, b_qaug[tile]], wr=[b_p])
                        expt[(hh, c)] = P.op("s", lambda e, hh=hh, lo=lo, w=w, ps=ps: e.activation(
                            pT[:, hh, lo:lo + w], ps[:, 0:w], ACT.Exp),
                            rd=[b_p], wr=[b_pT[hh]], reg=False)
                        b_p.r.append(expt[(hh, c)])
                        b_pT[hh].w = expt[(hh, c)]
                c0 = j // 4
                maskt = {}
                for hh in range(2):
                    maskt[hh] = P.op("v", lambda e, hh=hh, j=j: e.tensor_tensor(
                        pT[:, hh, 128 * j:128 * (j + 1)],
                        pT[:, hh, 128 * j:128 * (j + 1)], mask01[:], ALU.mult),
                        rd=[b_msk], wr=[], extra=[expt[(hh, c0)]], reg=False)
                for hh in range(2):
                    for (lo, w, c) in kchunks(j):
                        grp_start = (j == 0)
                        grp_stop = (j == min(15, 4 * c + 3))
                        do_inc = grp_stop or (c == 3)
                        ext = [expt[(hh, c)]]
                        if c == c0:
                            ext.append(maskt[hh])
                        tok = P.op("t", lambda e, j=j, g=g, hh=hh, lo=lo, w=w, c=c,
                                   st=grp_start, sp=grp_stop: e.matmul(
                            psf[2 + c][64 * hh:64 * hh + 33,
                                       lo - 512 * c:lo - 512 * c + w],
                            va[:, j, 33 * g:33 * g + 33],
                            pT[:, hh, lo:lo + w],
                            start=st, stop=sp, skip_group_check=True),
                            rd=[b_va],
                            wr=([b_ps[2 + c]] if (grp_start and hh == 0) else ()),
                            extra=ext, inc=do_inc, reg=False)
                        if do_inc:
                            b_pT[hh].r.append(tok)
                            if grp_stop and hh == 1:
                                P.reg_group(tok, rd=[b_va], wr=[b_ps[2 + c]])
                if j % 4 == 3:
                    c = j // 4
                    for hh in range(2):
                        h = 2 * pr + hh
                        hp = h % 4
                        pk = h // 4
                        rrow = 32 + 64 * hh
                        P.op("v", lambda e, hp=hp, pk=pk, c=c, hh=hh: e.tensor_copy(
                            z_sb[32 * hp:32 * hp + 32, pk, 512 * c:512 * (c + 1)],
                            psf[2 + c][64 * hh:64 * hh + 32, :]),
                            rd=[b_ps[2 + c]], wr=[b_z[pk]], extra=z_guard)
                        P.op("s", lambda e, c=c, rrow=rrow: e.activation(
                            rd_s[rrow:rrow + 1, :], psf[2 + c][rrow:rrow + 1, :], ACT.Ln),
                            rd=[b_ps[2 + c]], wr=[b_rd])
                        P.op("s", lambda e, rrow=rrow: e.activation(
                            rd_bf[0:1, :], rd_s[rrow:rrow + 1, :], ACT.Exp, scale=-1.0),
                            rd=[b_rd], wr=[b_rdbf])
                        fps, b_fp = psf[(0, 1, 6, 7)[(2 * c + hh) % 4]], b_ps[(0, 1, 6, 7)[(2 * c + hh) % 4]]
                        P.op("t", lambda e, fps=fps: e.matmul(
                            fps[0:32, :], ones1[0:1, 0:32], rd_bf[0:1, :],
                            start=True, stop=True),
                            rd=[b_rdbf, b_ones], wr=[b_fp])
                        P.op("v", lambda e, hp=hp, pk=pk, c=c, fps=fps: e.tensor_tensor(
                            z_sb[32 * hp:32 * hp + 32, pk, 512 * c:512 * (c + 1)],
                            z_sb[32 * hp:32 * hp + 32, pk, 512 * c:512 * (c + 1)],
                            fps[0:32, :], ALU.mult),
                            rd=[b_fp, b_z[pk]], wr=[b_z[pk]])


        for i in range(NQT):
            for dch in range(2):
                ps, b_p = next_ps()
                tok = None
                for pk in range(2):
                    tok = P.op("t", lambda e, i=i, dch=dch, pk=pk, ps=ps: e.matmul(
                        ps[:, :], z_sb[:, pk, 128 * i:128 * (i + 1)],
                        wo_pk[:, pk, 512 * dch:512 * (dch + 1)],
                        start=(pk == 0), stop=(pk == 1)),
                        rd=[b_z[pk], b_wopk], wr=([b_p] if pk == 0 else ()),
                        inc=(pk == 1), reg=False)
                P.reg_group(tok, rd=[b_z[0], b_z[1], b_wopk], wr=[b_p])
                P.op("v", lambda e, i=i, dch=dch, ps=ps: e.scalar_tensor_tensor(
                    x_res[:, i, 512 * dch:512 * (dch + 1)], ps[:, :], 1.0,
                    x_res[:, i, 512 * dch:512 * (dch + 1)], ALU.mult, ALU.add),
                    rd=[b_p, b_x[i]], wr=[b_x[i]])

        layernorm_phase()

        for i in range(NQT):
            ps, b_p = next_ps()
            tok = None
            for dc in range(NDC):
                tok = P.op("t", lambda e, i=i, dc=dc, ps=ps: e.matmul(
                    ps[:, 0:8], hT[:, dc, 128 * i:128 * (i + 1)], wg_bf[:, dc, :],
                    start=(dc == 0), stop=(dc == NDC - 1)),
                    rd=[b_hT[i], b_wg], wr=([b_p] if dc == 0 else ()),
                    inc=(dc == NDC - 1), reg=False)
            P.reg_group(tok, rd=[b_hT[i], b_wg], wr=[b_p])
            P.op("v", lambda e, ps=ps: e.reduce_max(
                r_sc[:], ps[:, 0:8], axis=AX, negate=True), rd=[b_p], wr=[b_rt])
            P.op("s", lambda e, ps=ps: e.activation(
                r_t1[:], ps[:, 0:8], ACT.Exp, bias=r_sc[:]),
                rd=[b_p, b_rt], wr=[b_rt])
            P.op("v", lambda e: e.reduce_sum(r_sc[:], r_t1[:], axis=AX),
                 rd=[b_rt], wr=[b_rt])
            P.op("v", lambda e: e.reciprocal(r_sc[:], r_sc[:]), rd=[b_rt], wr=[b_rt])
            P.op("v", lambda e: e.tensor_scalar_mul(r_t1[:], r_t1[:], r_sc[:]),
                 rd=[b_rt], wr=[b_rt])
            P.op("v", lambda e: e.reduce_max(r_sc[:], r_t1[:], axis=AX),
                 rd=[b_rt], wr=[b_rt])
            P.op("v", lambda e: e.tensor_scalar(r_t2[:], r_t1[:], r_sc[:], None, ALU.is_ge),
                 rd=[b_rt], wr=[b_rt])
            P.op("v", lambda e: e.tensor_tensor(r_t3[:], r_t1[:], r_t2[:], ALU.mult),
                 rd=[b_rt], wr=[b_rt])
            P.op("v", lambda e: e.tensor_tensor(r_t3[:], r_t1[:], r_t3[:], ALU.subtract),
                 rd=[b_rt], wr=[b_rt])
            P.op("v", lambda e: e.reduce_max(r_sc[:], r_t3[:], axis=AX),
                 rd=[b_rt], wr=[b_rt])
            P.op("v", lambda e: e.tensor_scalar(r_t3[:], r_t3[:], r_sc[:], None, ALU.is_ge),
                 rd=[b_rt], wr=[b_rt])
            P.op("v", lambda e: e.tensor_tensor(r_t2[:], r_t2[:], r_t3[:], ALU.add),
                 rd=[b_rt], wr=[b_rt])
            P.op("v", lambda e, i=i: e.tensor_tensor(
                comb[:, i, :], r_t1[:], r_t2[:], ALU.mult),
                rd=[b_rt], wr=[b_comb])

        def gate_dma(e_, buf):
            dsrc = dwgate if e_ < E else dwsg
            for dc in range(NDC):
                stg = dc % 2
                if e_ < E:
                    P.dma("q", lambda e, e_=e_, dc=dc, stg=stg, dsrc=dsrc: e.dma_start(
                        out=stage[:, stg, :],
                        in_=dsrc[e_, 128 * dc:128 * (dc + 1), :]),
                        wr=[b_stage[stg]], sem="E%d" % stg, serial=False)
                else:
                    P.dma("q", lambda e, dc=dc, stg=stg, dsrc=dsrc: e.dma_start(
                        out=stage[:, stg, :],
                        in_=dsrc[128 * dc:128 * (dc + 1), :]),
                        wr=[b_stage[stg]], sem="E%d" % stg, serial=False)
                P.op("g", lambda e, buf=buf, dc=dc, stg=stg: e.tensor_copy(
                    gate_bf[:, buf, dc, :], stage[:, stg, :]),
                    rd=[b_stage[stg]], wr=[b_gate[buf]], extra=st2_guard)

        def up_dma(e_):
            dsrc = dwup if e_ < E else dwsu
            for dc in range(NDC):
                stg = dc % 2
                if e_ < E:
                    P.dma("q", lambda e, e_=e_, dc=dc, stg=stg, dsrc=dsrc: e.dma_start(
                        out=stage[:, stg, :],
                        in_=dsrc[e_, 128 * dc:128 * (dc + 1), :]),
                        wr=[b_stage[stg]], sem="E%d" % stg, serial=False)
                else:
                    P.dma("q", lambda e, dc=dc, stg=stg, dsrc=dsrc: e.dma_start(
                        out=stage[:, stg, :],
                        in_=dsrc[128 * dc:128 * (dc + 1), :]),
                        wr=[b_stage[stg]], sem="E%d" % stg, serial=False)
                P.op("g", lambda e, dc=dc, stg=stg: e.tensor_copy(
                    up_bf[:, dc, :], stage[:, stg, :]),
                    rd=[b_stage[stg]], wr=[b_up])

        def down_dma(e_):
            for hc in range(4):
                for half in range(2):
                    stg = half
                    if e_ < E:
                        P.dma("q", lambda e, e_=e_, hc=hc, half=half, stg=stg: e.dma_start(
                            out=stage[:, stg, :],
                            in_=dwdown[e_, 128 * hc:128 * (hc + 1), 512 * half:512 * (half + 1)]),
                            wr=[b_stage[stg]], sem="E%d" % stg, serial=False)
                    else:
                        P.dma("q", lambda e, hc=hc, half=half, stg=stg: e.dma_start(
                            out=stage[:, stg, :],
                            in_=dwsd[128 * hc:128 * (hc + 1), 512 * half:512 * (half + 1)]),
                            wr=[b_stage[stg]], sem="E%d" % stg, serial=False)
                    P.op("v", lambda e, hc=hc, half=half, stg=stg: e.tensor_tensor(
                        down_bf[:, hc, 512 * half:512 * (half + 1)],
                        stage[:, stg, :],
                        gbv[:, 512 * half:512 * (half + 1)], ALU.mult),
                        rd=[b_stage[stg], b_gbv], wr=[b_down])

        st2_guard = []
        for bb in b_st2:
            st2_guard.extend(bb.r)
            if bb.w:
                st2_guard.append(bb.w)

        gate_dma(0, 0)
        up_dma(0)
        for e_ in range(E + 1):
            buf = e_ % 2
            down_dma(e_)
            if e_ + 1 <= E:
                gate_dma(e_ + 1, (e_ + 1) % 2)
            for hc in range(4):
                for qc in range(4):
                    psg, b_pg = next_ps()
                    tok = None
                    for dc in range(NDC):
                        tok = P.op("t", lambda e, buf=buf, hc=hc, qc=qc, dc=dc, psg=psg: e.matmul(
                            psg[:, :], gate_bf[:, buf, dc, 128 * hc:128 * (hc + 1)],
                            hT[:, dc, 512 * qc:512 * (qc + 1)],
                            start=(dc == 0), stop=(dc == NDC - 1)),
                            rd=[b_gate[buf]] + [b_hT[4 * qc + u] for u in range(4)],
                            wr=([b_pg] if dc == 0 else ()),
                            inc=(dc == NDC - 1), reg=False)
                    P.reg_group(tok, rd=[b_gate[buf]], wr=[b_pg])
                    ssl = qc % 2
                    P.op("s", lambda e, ssl=ssl, psg=psg: e.activation(
                        silu_s[:, ssl, :], psg[:, :], ACT.Silu),
                        rd=[b_pg], wr=[b_silu[ssl]])
                    psu, b_pu = next_ps()
                    tok = None
                    for dc in range(NDC):
                        tok = P.op("t", lambda e, hc=hc, qc=qc, dc=dc, psu=psu: e.matmul(
                            psu[:, :], up_bf[:, dc, 128 * hc:128 * (hc + 1)],
                            hT[:, dc, 512 * qc:512 * (qc + 1)],
                            start=(dc == 0), stop=(dc == NDC - 1)),
                            rd=[b_up] + [b_hT[4 * qc + u] for u in range(4)],
                            wr=([b_pu] if dc == 0 else ()),
                            inc=(dc == NDC - 1), reg=False)
                    P.reg_group(tok, rd=[b_up], wr=[b_pu])
                    P.op("v", lambda e, hc=hc, qc=qc, ssl=ssl, psu=psu: e.tensor_tensor(
                        act_e[:, hc, 512 * qc:512 * (qc + 1)],
                        psu[:, :], silu_s[:, ssl, :], ALU.mult),
                        rd=[b_pu, b_silu[ssl]], wr=[b_act])
            if e_ + 1 <= E:
                up_dma(e_ + 1)
            for i in range(NQT):
                for dch in range(2):
                    psd, b_pd = next_ps()
                    tok = None
                    for hc in range(4):
                        tok = P.op("t", lambda e, i=i, dch=dch, hc=hc, psd=psd: e.matmul(
                            psd[:, :], act_e[:, hc, 128 * i:128 * (i + 1)],
                            down_bf[:, hc, 512 * dch:512 * (dch + 1)],
                            start=(hc == 0), stop=(hc == 3)),
                            rd=[b_act, b_down],
                            wr=([b_pd] if hc == 0 else ()),
                            inc=(hc == 3), reg=False)
                    P.reg_group(tok, rd=[b_act, b_down], wr=[b_pd])
                    if e_ < E:
                        P.op("v", lambda e, i=i, dch=dch, e_=e_, psd=psd: e.scalar_tensor_tensor(
                            x_res[:, i, 512 * dch:512 * (dch + 1)], psd[:, :],
                            comb[:, i, e_:e_ + 1],
                            x_res[:, i, 512 * dch:512 * (dch + 1)],
                            ALU.mult, ALU.add),
                            rd=[b_pd, b_comb, b_x[i]], wr=[b_x[i]])
                    else:
                        P.op("v", lambda e, i=i, dch=dch, psd=psd: e.scalar_tensor_tensor(
                            x_res[:, i, 512 * dch:512 * (dch + 1)], psd[:, :], 1.0,
                            x_res[:, i, 512 * dch:512 * (dch + 1)],
                            ALU.mult, ALU.add),
                            rd=[b_pd, b_x[i]], wr=[b_x[i]])

        for i in range(NQT):
            P.dma("s", lambda e, i=i: e.dma_start(
                out=dy[128 * i:128 * (i + 1), :], in_=x_res[:, i, :]),
                rd=[b_x[i]], sem="Do", serial=False)

        with nc.allow_low_precision(
                "bf16 matmul operands with fp32 accumulation per problem tolerance"):
            P.finalize()
    return nc


def _get_program():
    if "nc" not in _CACHE:
        _CACHE["nc"] = build_program()
    return _CACHE["nc"]


def kernel(**inputs):
    nc = _get_program()
    f32 = lambda a: np.ascontiguousarray(np.asarray(a, dtype=np.float32))
    shared = {
        "wq": f32(inputs["L0_attn_wq"]),
        "wk_down": f32(inputs["L0_attn_wk_down"]),
        "wv_down": f32(inputs["L0_attn_wv_down"]),
        "wk_up": f32(inputs["L0_attn_wk_up"]),
        "wv_up": f32(inputs["L0_attn_wv_up"]),
        "wo": f32(inputs["L0_attn_wo"]),
        "w_g": f32(inputs["L0_ffn_W_g"]),
        "w_gate": f32(inputs["L0_ffn_W_gate"]),
        "w_up": f32(inputs["L0_ffn_W_up"]),
        "w_down": f32(inputs["L0_ffn_W_down"]),
        "w_sg": f32(inputs["L0_ffn_W_shared_gate"]),
        "w_su": f32(inputs["L0_ffn_W_shared_up"]),
        "w_sd": f32(inputs["L0_ffn_W_shared_down"]),
        "wmod": f32(inputs["L0_wmod"]),
    }
    x = f32(inputs["L0_attn_x"])
    c = f32(inputs["L0_c"])
    in_maps = []
    for core in range(8):
        m = dict(shared)
        m["x"] = np.ascontiguousarray(x[core])
        m["c_row"] = np.ascontiguousarray(c[core])
        in_maps.append(m)
    res = run_bass_kernel_spmd(nc, in_maps, core_ids=list(range(8)))
    return np.stack([res.results[i]["y"] for i in range(8)], axis=0)
